# version 10 (final)
# baseline (speedup 1.0000x reference)
import numpy as np
import concourse.bass as bass
import concourse.mybir as mybir
from concourse.bass_utils import run_bass_kernel_spmd
from contextlib import ExitStack

F32 = mybir.dt.float32
F16 = mybir.dt.float16
U32 = mybir.dt.uint32
I32 = mybir.dt.int32
AF = mybir.ActivationFunctionType
ALU = mybir.AluOpType
AXX = mybir.AxisListType.X

NCORE = 8
S = 1024
H = 4096
INTER = 11008
I2 = 2 * INTER
NPAIR = 88
NKT = 32
NGRP = 8
SCALE_K = 4096.0

_DBG = False
_RACE = True
_PHASES = 6
_P3SUB = 3
_SKIP12 = False


def pair_info(t):
    k, b = t // 11, t % 11
    w = 128 if b < 10 else 96
    co = 2752 * k + 256 * b
    fo = 1376 * k + 128 * b
    return k, b, w, co, fo


PAIRS = [pair_info(t) for t in range(NPAIR)]


def conv_base(qb, k, it):
    return 3968 - 128 * it + 1024 * qb + k


def conv_valid(qb, it):
    return it >= 8 * qb


def slabs_of_grp(grp):
    out = []
    for qb in range(4):
        if 4 * grp + 3 < 8 * qb:
            continue
        for k in range(8):
            out.append((qb, k))
    return out


NSLAB_CUM = [0]
for _g in range(NGRP):
    NSLAB_CUM.append(NSLAB_CUM[-1] + len(slabs_of_grp(_g)))


def host_prep(inputs):
    x = np.ascontiguousarray(np.asarray(inputs["L0_attn_u"]).reshape(8192, H))
    xsh = [np.ascontiguousarray(x[1024 * c:1024 * (c + 1)]) for c in range(NCORE)]
    cols = []
    for t in range(NPAIR):
        k, b, w, co, fo = PAIRS[t]
        c = 8 * (128 * b + np.arange(w)) + k
        cols.append(c)
    colperm = np.concatenate([np.concatenate([c, INTER + c]) for c in cols])
    d = dict(
        qw_gu=np.ascontiguousarray(np.asarray(inputs["L0_ffn_qweight_gu"]).astype(np.uint32)[:, colperm]),
        sc_gu=np.ascontiguousarray(np.asarray(inputs["L0_ffn_scales_gu"]).astype(np.float32)[:, colperm]),
        qz_gu=np.ascontiguousarray(np.asarray(inputs["L0_ffn_qzeros_gu"]).astype(np.uint32)),
        gi_gu=np.ascontiguousarray(np.asarray(inputs["L0_ffn_g_idx_gu"]).astype(np.int32)),
        qw_d=np.ascontiguousarray(np.asarray(inputs["L0_ffn_qweight_down"]).astype(np.uint32)),
        sc_d=np.ascontiguousarray(np.asarray(inputs["L0_ffn_scales_down"]).astype(np.float32)),
        qz_d=np.ascontiguousarray(np.asarray(inputs["L0_ffn_qzeros_down"]).astype(np.uint32)),
        gi_d=np.ascontiguousarray(np.asarray(inputs["L0_ffn_g_idx_down"]).astype(np.int32)),
        lam=np.asarray(inputs["L0_attn_Lambda_ri"]).astype(np.float32),
        pri=np.asarray(inputs["L0_attn_p_ri"]).astype(np.float32),
        qri=np.asarray(inputs["L0_attn_q_ri"]).astype(np.float32),
        vc=np.ascontiguousarray(np.asarray(inputs["L0_attn_Vc_ri"]).astype(np.float32).reshape(64, 128)),
        ct=np.ascontiguousarray(np.asarray(inputs["L0_attn_Ct_ri"]).astype(np.float32).reshape(64, 2)),
        bp=np.ascontiguousarray(np.asarray(inputs["L0_attn_Bp"]).astype(np.float32).reshape(64, 1)),
        dp=np.asarray(inputs["L0_attn_D"]).astype(np.float32).reshape(1, 1),
        ls=np.asarray(inputs["L0_attn_log_step"]).astype(np.float32).reshape(1, 1),
    )
    return xsh, d


def make_consts():
    L = H
    idx = (np.arange(L, dtype=np.float32) / np.float32(L)).astype(np.float32)
    Om = np.exp(np.complex64(-2j * np.pi) * idx.astype(np.complex64)).astype(np.complex64)
    c1 = ((np.complex64(1) - Om) / (np.complex64(1) + Om)).astype(np.complex64).reshape(64, 64)
    c2 = (np.complex64(2) / (np.complex64(1) + Om)).astype(np.complex64).reshape(64, 64)
    a = np.arange(64)
    w64 = np.exp(2j * np.pi * np.outer(a, a) / 64.0)
    tw = np.exp(2j * np.pi * np.outer(a, a) / 4096.0)
    C = dict(
        c1r=c1.real.astype(np.float32), c1i=c1.imag.astype(np.float32),
        c2r=c2.real.astype(np.float32), c2i=c2.imag.astype(np.float32),
        w64r=w64.real.astype(np.float32), w64i=w64.imag.astype(np.float32),
        w64in=(-w64.imag).astype(np.float32),
        twr=tw.real.astype(np.float32), twi=tw.imag.astype(np.float32),
        j64=np.eye(64)[::-1].copy().astype(np.float32),
        id128=np.eye(128).astype(np.float16),
        id128f=np.eye(128).astype(np.float32),
        id64f=np.eye(64).astype(np.float32),
        ones64=np.ones((1, 64), dtype=np.float32),
        grow32=np.broadcast_to(np.arange(32, dtype=np.float32), (128, 32)).copy(),
        epsln=np.full((128, 1), 1e-5, dtype=np.float32),
        invh=np.full((128, 1), 1.0 / 4096.0, dtype=np.float32),
        nhalf=np.full((128, 1), -0.5, dtype=np.float32),
        grow86=np.broadcast_to(np.arange(86, dtype=np.float32), (128, 86)).copy(),
    )
    return C


def apx(view, off, dims):
    pd = list(view.ap)[0]
    return bass.AP(view.tensor, view.offset + off, [list(pd)] + [list(dd) for dd in dims])


def build_nc():
    nc = bass.Bass("TRN2", target_bir_lowering=False, detect_race_conditions=_RACE)
    C = make_consts()
    cn = {k: nc.inline_tensor(v, "c_" + k) for k, v in C.items()}

    EIN = lambda n, sh, dt: nc.dram_tensor(n, sh, dt, kind="ExternalInput")
    t_x = EIN("x", [S, H], F32)
    t_qwgu = EIN("qw_gu", [512, I2], U32)
    t_scgu = EIN("sc_gu", [32, I2], F32)
    t_qzgu = EIN("qz_gu", [32, 2752], U32)
    t_gigu = EIN("gi_gu", [H], I32)
    t_qwd = EIN("qw_d", [1376, H], U32)
    t_scd = EIN("sc_d", [86, H], F32)
    t_qzd = EIN("qz_d", [86, 512], U32)
    t_gid = EIN("gi_d", [INTER], I32)
    t_lam = EIN("lam", [64, 2], F32)
    t_pri = EIN("pri", [64, 2], F32)
    t_qri = EIN("qri", [64, 2], F32)
    t_vc = EIN("vc", [64, 128], F32)
    t_ct = EIN("ct", [64, 2], F32)
    t_bp = EIN("bp", [64, 1], F32)
    t_dp = EIN("dp", [1, 1], F32)
    t_ls = EIN("ls", [1, 1], F32)

    t_out = nc.dram_tensor("out", [S, H], F32, kind="ExternalOutput")
    t_kr = nc.dram_tensor("kr_dram", [8191], F16)
    t_h = nc.dram_tensor("h_dram", [NPAIR * 128 * S], F16)
    t_wd = nc.dram_tensor("wd_dram", [NGRP * NPAIR * 128 * 512], F16)
    t_szp = nc.dram_tensor("szp_dram", [32, I2], F16)

    dbg = {}
    if _DBG:
        dbg["xnt"] = nc.dram_tensor("dbg_xnt", [128, 32 * S], F16, kind="ExternalOutput")
        dbg["kr"] = nc.dram_tensor("dbg_kr", [1, 8191], F16, kind="ExternalOutput")
        dbg["h"] = nc.dram_tensor("dbg_h", [1, 128 * S], F16, kind="ExternalOutput")
        dbg["xgn"] = nc.dram_tensor("dbg_xgn", [32, S], F16, kind="ExternalOutput")
        dbg["xgdn"] = nc.dram_tensor("dbg_xgdn", [86, S], F16, kind="ExternalOutput")
        dbg["szp"] = nc.dram_tensor("dbg_szp", [32, I2], F16, kind="ExternalOutput")

    TOP = ExitStack()
    sbT = lambda n, sh, dt, side="left": TOP.enter_context(nc.sbuf_tensor(n, sh, dt, side=side))
    ps = lambda n, sh, dt: TOP.enter_context(nc.psum_tensor(n, sh, dt))
    sem = lambda n: TOP.enter_context(nc.semaphore(n))

    xnt = sbT("xnt", [128, 32 * S], F16)
    xgn = sbT("xgn", [32, S], F16)
    xgdn = sbT("xgdn", [86, S], F16)
    id128 = sbT("id128", [128, 128], F16)
    id128f = sbT("id128f", [128, 128], F32)
    c_eps = sbT("sb_eps", [128, 1], F32)
    c_invh = sbT("sb_invh", [128, 1], F32)
    c_nhalf = sbT("sb_nhalf", [128, 1], F32)
    sml = sbT("sml", [128, 1420], F32)
    smf = sbT("smf", [64, 128], F16)
    g32 = sbT("g32", [128, 32], F32)
    g86 = sbT("g86", [128, 86], F32)

    RS = ExitStack()
    sbR = lambda n, sh, dt: RS.enter_context(nc.sbuf_tensor(n, sh, dt, side="right"))
    scgu16 = sbR("scgu16", [32, I2], F16)
    oPTgu = sbR("oPTgu", [32, H], F16)
    oPgu = sbR("oPgu", [128, 32 * 32], F16)
    oPd = sbR("oPd", [128, NPAIR * 86], F16)

    p = [ps(f"p{i}", [128, 512], F32) for i in range(7)]
    pT16 = ps("pT16", [128, 1024], F16)

    s_ld = sem("s_ld"); s_kc = sem("s_kc"); s_tb = sem("s_tb")
    s_kcd = sem("s_kcd"); s_tbd = sem("s_tbd"); s_p5d = sem("s_p5d")
    s_lx = sem("s_lx"); s_la = sem("s_la"); s_lv = sem("s_lv")
    s_lr = sem("s_lr"); s_ln = sem("s_ln"); s_lp = sem("s_lp"); s_lc = sem("s_lc")
    s_xg = sem("s_xg")
    g_qw = sem("g_qw"); g_sce = sem("g_sce"); g_w = sem("g_w"); g_ld = sem("g_ld")
    g_pe = sem("g_pe"); g_act = sem("g_act"); g_h = sem("g_h"); g_xgd = sem("g_xgd")
    g_hst = sem("g_hst"); g_szp = sem("g_szp")
    s_p5 = sem("s_p5")
    d_qw = sem("d_qw"); d_wl = sem("d_wl"); d_h = sem("d_h"); d_sce = sem("d_sce")
    d_w = sem("d_w"); d_cons = sem("d_cons"); d_ws = sem("d_ws"); d_sl = sem("d_sl")
    d_cv = sem("d_cv"); d_pe = sem("d_pe"); d_ev = sem("d_ev"); d_tp = sem("d_tp")
    d_tc = sem("d_tc"); d_oc = sem("d_oc")
    s_xr = sem("s_xr"); s_out = sem("s_out")

    ROWS = dict(ones64=0, lamr=1, lami=2, pr=3, pi_=4, qr=5, qi=6,
                ctr=7, cti=8, bp=9, Brr=10, Bir=11,
                v0=12, v1=13, v2=14, v3=15, v4=16, v5=17, v6=18, v7=19,
                t1=20, t2=21)

    def pr_(name):
        c = ROWS[name]
        return sml[0:1, 64 * c:64 * c + 64]

    SC_ = dict(dp=1408, ls=1409, step=1410, istep=1411, tsc=1412, ds=1413)

    def pv(name):
        c = SC_[name]
        return sml[0:1, c:c + 1]

    v_tsc64 = sml[0:64, 1414:1415]
    v_Br = sml[0:64, 1416:1417]
    v_Bi = sml[0:64, 1417:1418]

    def vterm(i):
        return pr_("v%d" % i)

    class DrainAfter:
        def __init__(self, q):
            self._q = q

        def __getattr__(self, name):
            attr = getattr(self._q, name)
            if name in ("wait_ge", "drain", "nop") or not callable(attr):
                return attr

            def wrap(*a, **k):
                r = attr(*a, **k)
                self._q.drain()
                return r
            return wrap

    def run_chain(E, q, OPS, sC, sD, base_c, base_d):
        ncm, ndm = base_c, base_d
        for i, (en, fn) in enumerate(OPS):
            if en == E:
                if ncm > base_c or E == "gpsimd":
                    q.wait_ge(sC, 16 * ncm)
                if ndm > base_d or E != "gpsimd":
                    q.wait_ge(sD, 16 * ndm)
                r = fn(q if en in ("gpsimd", "tensor") else DrainAfter(q))
                if en == "gpsimd":
                    if isinstance(r, tuple):
                        r = r[-1]
                    if r is None:
                        q.nop().then_inc(sD, 16)
                    else:
                        r.then_inc(sD, 16)
                else:
                    q.drain().then_inc(sC, 16)
            if en == "gpsimd":
                ndm += 1
            else:
                ncm += 1
        return ncm, ndm

    def chain_totals(OPS):
        nd = sum(1 for en, _ in OPS if en == "gpsimd")
        return len(OPS) - nd, nd


    P1 = ExitStack()
    sb1 = lambda n, sh, dt: P1.enter_context(nc.sbuf_tensor(n, sh, dt))
    kcst = sb1("kcst", [64, 64 * 11], F32)
    sA = sb1("sA", [64, 2048], F32)
    sB = sb1("sB", [64, 2048], F32)
    sC = sb1("sC", [64, 2048], F32)
    sD = sb1("sD", [64, 2048], F32)
    sE = sb1("sE", [64, 64 * 12], F32)
    vcb = sb1("vcb", [64, 128], F32)
    b_ri = sb1("b_ri", [64, 8], F32)

    def kc(i):
        return kcst[0:64, 64 * i:64 * i + 64]

    (KC_C1R, KC_C1I, KC_C2R, KC_C2I, KC_W64R, KC_W64I, KC_W64IN, KC_TWR,\
     KC_TWI, KC_J64, KC_ID64) = range(11)

    def se(i):
        return sE[0:64, 64 * i:64 * i + 64]

    P0_LOADS = [
        (kc(KC_C1R), cn["c1r"], 0), (kc(KC_C1I), cn["c1i"], 0),
        (kc(KC_C2R), cn["c2r"], 0), (kc(KC_C2I), cn["c2i"], 0),
        (kc(KC_W64R), cn["w64r"], 0), (kc(KC_W64I), cn["w64i"], 0),
        (kc(KC_W64IN), cn["w64in"], 0), (kc(KC_TWR), cn["twr"], 0),
        (kc(KC_TWI), cn["twi"], 0), (kc(KC_J64), cn["j64"], 0),
        (kc(KC_ID64), cn["id64f"], 0),
        (id128[:], cn["id128"], 0), (id128f[:], cn["id128f"], 0),
        (c_eps[:], cn["epsln"], 0), (c_nhalf[:], cn["nhalf"], 0),
        (c_invh[:], cn["invh"], 0),
        (pr_("ones64"), cn["ones64"], 0),
        (g32[:], cn["grow32"], 0), (g86[:], cn["grow86"], 0),
        (b_ri[0:64, 0:2], t_lam, 0),
        (b_ri[0:64, 2:4], t_pri, 0),
        (b_ri[0:64, 4:6], t_qri, 0),
        (b_ri[0:64, 6:8], t_ct, 0),
        (pr_("bp"), bass.AP(t_bp, 0, [[64, 1], [1, 64]]), 0),
        (pv("dp"), t_dp, 0), (pv("ls"), t_ls, 0),
        (vcb[:], t_vc, 0),
    ]
    N_P0 = len(P0_LOADS)

    KOPS = []
    kop = lambda en, fn: KOPS.append((en, fn))

    kop("scalar", lambda e: e.activation(sml[0:1, 252:253], c_eps[0:1, 0:1], AF.Exp))
    kop("gpsimd", lambda e: e.dma_start(out=sml[0:1, 253:254], in_=bass.AP(t_dp, 0, [[1, 1], [1, 1]])))
    kop("gpsimd", lambda e: e.dma_start(out=sml[0:1, 254:255], in_=bass.AP(t_dp, 0, [[1, 1], [1, 1]])))
    for _c, _nm in enumerate(["lamr", "lami", "pr", "pi_", "qr", "qi", "ctr", "cti"]):
        kop("tensor", lambda e, c=_c: e.matmul(p[0][0:1, 0:64], b_ri[0:64, c:c + 1],
                                               kc(KC_ID64), start=True, stop=True))
        kop("vector", lambda e, nm=_nm: e.tensor_copy(pr_(nm), p[0][0:1, 0:64]))
    kop("scalar", lambda e: e.activation(pv("step"), pv("ls"), AF.Exp))
    kop("vector", lambda e: e.reciprocal(pv("istep"), pv("step")))
    kop("vector", lambda e: e.tensor_scalar(pv("tsc"), pv("istep"), 2.0, None, ALU.mult))
    kop("tensor", lambda e: e.matmul(p[0][0:64, 0:1], pr_("ones64"), pv("tsc"), start=True, stop=True))
    kop("vector", lambda e: e.tensor_copy(v_tsc64, p[0][0:64, 0:1]))
    kop("vector", lambda e: e.tensor_scalar(se(0), kc(KC_C1R), v_tsc64, None, ALU.mult))
    kop("vector", lambda e: e.tensor_scalar(se(1), kc(KC_C1I), v_tsc64, None, ALU.mult))
    kop("tensor", lambda e: e.matmul(p[1][0:64, 0:64], pr_("ones64"), pr_("bp"), start=True, stop=True))
    kop("vector", lambda e: e.tensor_tensor(
        se(2), vcb[:].rearrange("p (m r) -> p m r", m=64, r=2)[:, :, 0:1].squeeze(2),
        p[1][0:64, 0:64], ALU.mult))
    kop("vector", lambda e: e.reduce_sum(v_Br.unsqueeze(2),
                                         se(2).rearrange("p (a b) -> p a b", a=1, b=64), axis=AXX))
    kop("vector", lambda e: e.tensor_tensor(
        se(2), vcb[:].rearrange("p (m r) -> p m r", m=64, r=2)[:, :, 1:2].squeeze(2),
        p[1][0:64, 0:64], ALU.mult))
    kop("vector", lambda e: e.reduce_sum(v_Bi.unsqueeze(2),
                                         se(2).rearrange("p (a b) -> p a b", a=1, b=64), axis=AXX))
    kop("tensor", lambda e: e.matmul(p[0][0:1, 0:64], v_Br, kc(KC_ID64), start=True, stop=True))
    kop("vector", lambda e: e.tensor_copy(pr_("Brr"), p[0][0:1, 0:64]))
    kop("tensor", lambda e: e.matmul(p[0][0:1, 0:64], v_Bi, kc(KC_ID64), start=True, stop=True))
    kop("vector", lambda e: e.tensor_copy(pr_("Bir"), p[0][0:1, 0:64]))

    def cmul_conj(dr_, di_, ar, ai, br, bi):
        def f(e):
            e.tensor_tensor(pr_("t1"), ar, br, ALU.mult)
            e.tensor_tensor(pr_("t2"), ai, bi, ALU.mult)
            e.tensor_tensor(dr_, pr_("t1"), pr_("t2"), ALU.add)
            e.tensor_tensor(pr_("t1"), ar, bi, ALU.mult)
            e.tensor_tensor(pr_("t2"), ai, br, ALU.mult)
            return e.tensor_tensor(di_, pr_("t1"), pr_("t2"), ALU.subtract)
        return f

    kop("vector", cmul_conj(vterm(0), vterm(1), pr_("ctr"), pr_("cti"), pr_("Brr"), pr_("Bir")))
    kop("vector", cmul_conj(vterm(2), vterm(3), pr_("ctr"), pr_("cti"), pr_("pr"), pr_("pi_")))
    kop("vector", cmul_conj(vterm(4), vterm(5), pr_("qr"), pr_("qi"), pr_("Brr"), pr_("Bir")))
    kop("vector", cmul_conj(vterm(6), vterm(7), pr_("qr"), pr_("qi"), pr_("pr"), pr_("pi_")))

    def bcast(dst, row):
        return lambda e: e.matmul(dst, pr_("ones64"), row, start=True, stop=True)

    kop("tensor", bcast(p[2][0:64, 0:64], pr_("lamr")))
    kop("tensor", bcast(p[3][0:64, 0:64], pr_("lami")))
    kop("vector", lambda e: e.tensor_copy(se(2), p[2][0:64, 0:64]))
    kop("vector", lambda e: e.tensor_copy(se(3), p[3][0:64, 0:64]))

    def bc_nh(v, h):
        return v[:, 32 * h:32 * h + 32].unsqueeze(2).broadcast_to([64, 32, 64])

    def bc_bh(v):
        return v.rearrange("p (a b) -> p a b", a=1, b=64).broadcast_to([64, 32, 64])

    def h3(buf):
        return buf[0:64, :].rearrange("p (a b) -> p a b", a=32, b=64)

    for h in range(2):
        kop("vector", lambda e, h=h: e.tensor_tensor(h3(sA), bc_nh(se(0), h), bc_bh(se(2)), ALU.subtract))
        kop("vector", lambda e, h=h: e.tensor_tensor(h3(sB), bc_nh(se(1), h), bc_bh(se(3)), ALU.subtract))
        kop("vector", lambda e: e.tensor_tensor(sC[0:64, :], sA[0:64, :], sA[0:64, :], ALU.mult))
        kop("vector", lambda e: e.tensor_tensor(sD[0:64, :], sB[0:64, :], sB[0:64, :], ALU.mult))
        kop("vector", lambda e: e.tensor_tensor(sC[0:64, :], sC[0:64, :], sD[0:64, :], ALU.add))
        kop("vector", lambda e: e.reciprocal(sD[0:64, :], sC[0:64, :]))
        kop("vector", lambda e: e.tensor_tensor(sC[0:64, :], sA[0:64, :], sD[0:64, :], ALU.mult))
        kop("vector", lambda e: e.scalar_tensor_tensor(sD[0:64, :], sB[0:64, :], -1.0, sD[0:64, :],
                                                       ALU.mult, ALU.mult))

        def kterm(i, h):
            krv = se(4 + 2 * i); kiv = se(5 + 2 * i)
            def f(e):
                e.tensor_tensor(h3(sA), h3(sC), bc_bh(p[4][0:64, 0:64]), ALU.mult)
                e.tensor_tensor(h3(sB), h3(sD), bc_bh(p[5][0:64, 0:64]), ALU.mult)
                e.tensor_tensor(sA[0:64, :], sA[0:64, :], sB[0:64, :], ALU.subtract)
                e.reduce_sum(krv[:, 32 * h:32 * h + 32].unsqueeze(2), h3(sA), axis=AXX)
                e.tensor_tensor(h3(sA), h3(sC), bc_bh(p[5][0:64, 0:64]), ALU.mult)
                e.tensor_tensor(h3(sB), h3(sD), bc_bh(p[4][0:64, 0:64]), ALU.mult)
                e.tensor_tensor(sA[0:64, :], sA[0:64, :], sB[0:64, :], ALU.add)
                return e.reduce_sum(kiv[:, 32 * h:32 * h + 32].unsqueeze(2), h3(sA), axis=AXX)
            return f

        for i in range(4):
            kop("tensor", bcast(p[4][0:64, 0:64], vterm(2 * i)))
            kop("tensor", bcast(p[5][0:64, 0:64], vterm(2 * i + 1)))
            kop("vector", kterm(i, h))

    W_R = se(0); W_I = se(1); ZR = se(2); RZ = se(3)
    TA = lambda: sA[0:64, 0:64]
    TB = lambda: sB[0:64, 0:64]

    def at_roots2(e):
        k00r, k00i = se(4), se(5); k01r, k01i = se(6), se(7)
        k10r, k10i = se(8), se(9); k11r, k11i = se(10), se(11)
        e.tensor_scalar(ZR, k11r, 1.0, None, ALU.add)
        e.tensor_tensor(RZ, ZR, ZR, ALU.mult)
        e.tensor_tensor(TA(), k11i, k11i, ALU.mult)
        e.tensor_tensor(RZ, RZ, TA(), ALU.add)
        e.reciprocal(RZ, RZ)
        e.tensor_tensor(W_R, k01r, k10r, ALU.mult)
        e.tensor_tensor(TA(), k01i, k10i, ALU.mult)
        e.tensor_tensor(W_R, W_R, TA(), ALU.subtract)
        e.tensor_tensor(W_I, k01r, k10i, ALU.mult)
        e.tensor_tensor(TA(), k01i, k10r, ALU.mult)
        e.tensor_tensor(W_I, W_I, TA(), ALU.add)
        e.tensor_tensor(TA(), W_R, ZR, ALU.mult)
        e.tensor_tensor(TB(), W_I, k11i, ALU.mult)
        e.tensor_tensor(TA(), TA(), TB(), ALU.add)
        e.tensor_tensor(TA(), TA(), RZ, ALU.mult)
        e.tensor_tensor(TB(), W_I, ZR, ALU.mult)
        e.tensor_tensor(W_R, W_R, k11i, ALU.mult)
        e.tensor_tensor(TB(), TB(), W_R, ALU.subtract)
        e.tensor_tensor(TB(), TB(), RZ, ALU.mult)
        e.tensor_tensor(TA(), k00r, TA(), ALU.subtract)
        e.tensor_tensor(TB(), k00i, TB(), ALU.subtract)
        e.tensor_tensor(W_R, kc(KC_C2R), TA(), ALU.mult)
        e.tensor_tensor(W_I, kc(KC_C2I), TB(), ALU.mult)
        e.tensor_tensor(se(4), W_R, W_I, ALU.subtract)
        e.tensor_tensor(W_R, kc(KC_C2R), TB(), ALU.mult)
        e.tensor_tensor(W_I, kc(KC_C2I), TA(), ALU.mult)
        return e.tensor_tensor(se(5), W_R, W_I, ALU.add)

    kop("vector", at_roots2)
    kop("tensor", lambda e: (
        e.matmul(p[2][0:64, 0:64], se(4), kc(KC_W64R), start=True, stop=False),
        e.matmul(p[2][0:64, 0:64], se(5), kc(KC_W64IN), start=False, stop=True),
        e.matmul(p[3][0:64, 0:64], se(4), kc(KC_W64I), start=True, stop=False),
        e.matmul(p[3][0:64, 0:64], se(5), kc(KC_W64R), start=False, stop=True)))
    kop("vector", lambda e: (e.tensor_copy(se(6), p[2][0:64, 0:64]),
                             e.tensor_copy(se(7), p[3][0:64, 0:64])))
    kop("vector", lambda e: (
        e.tensor_tensor(se(8), se(6), kc(KC_TWR), ALU.mult),
        e.tensor_tensor(se(9), se(7), kc(KC_TWI), ALU.mult),
        e.tensor_tensor(se(8), se(8), se(9), ALU.subtract),
        e.tensor_tensor(se(9), se(6), kc(KC_TWI), ALU.mult),
        e.tensor_tensor(se(10), se(7), kc(KC_TWR), ALU.mult),
        e.tensor_tensor(se(9), se(9), se(10), ALU.add)))
    kop("tensor", lambda e: (
        e.matmul(p[2][0:64, 0:64], se(8), kc(KC_W64R), start=True, stop=False),
        e.matmul(p[2][0:64, 0:64], se(9), kc(KC_W64IN), start=False, stop=True)))
    kop("vector", lambda e: e.tensor_copy(se(6), p[2][0:64, 0:64]))
    kop("vector", lambda e: (
        e.tensor_scalar(pv("ds"), pv("dp"), SCALE_K, None, ALU.mult),
        e.tensor_tensor(se(6)[0:1, 0:1], se(6)[0:1, 0:1], pv("ds"), ALU.add)))
    kop("vector", lambda e: e.tensor_copy(se(7), apx(se(6), 63, [[-1, 64]])))
    kop("tensor", lambda e: e.transpose(p[2][0:64, 0:64], se(7), kc(KC_J64)))
    kop("vector", lambda e: e.tensor_copy(smf[0:64, 0:64], p[2][0:64, 0:64]))
    kop("vector", lambda e: e.memset(smf[0:64, 64:128], 0.0))
    kop("gpsimd", lambda e: e.dma_start(out=bass.AP(t_kr, 0, [[64, 64], [1, 64]]), in_=smf[0:64, 0:64]))
    kop("gpsimd", lambda e: e.dma_start(out=bass.AP(t_kr, 4096, [[63, 64], [1, 63]]), in_=smf[0:64, 64:127]))
    kop("gpsimd", lambda e: e.dma_start(out=bass.AP(t_kr, 4096 + 4032, [[63, 1], [1, 63]]), in_=smf[0:1, 64:127]))
    if _DBG:
        kop("gpsimd", lambda e: e.dma_start(out=dbg["kr"].ap(), in_=bass.AP(t_kr, 0, [[8191, 1], [1, 8191]])))
    N_KOPS = len(KOPS)

    def emit_p0_only():
        with nc.Block() as block:
            @block.sync
            def _(sy):
                for dst, s_, slow in P0_LOADS:
                    if slow:
                        sy.nop().then_inc(s_ld, 16)
                        continue
                    a = s_ if isinstance(s_, bass.AP) else s_.ap()
                    sy.dma_start(out=dst, in_=a).then_inc(s_ld, 16)

    def emit_p0p1():
        with nc.Block() as block:
            @block.sync
            def _(sy):
                for dst, s_, slow in P0_LOADS:
                    a = s_ if isinstance(s_, bass.AP) else s_.ap()
                    sy.dma_start(out=dst, in_=a,
                                 allow_slow_non_contiguous=bool(slow)).then_inc(s_ld, 16)
            for en in ["gpsimd", "scalar", "tensor", "vector"]:
                def mk(en):
                    def f(q):
                        q.wait_ge(s_ld, 16 * N_P0)
                        run_chain(en, q, KOPS, s_kc, s_kcd, 0, 0)
                    return f
                getattr(block, en)(mk(en))

    def emit_p2():
        with ExitStack() as P2a:
            sb2 = lambda n, sh, dt: P2a.enter_context(nc.sbuf_tensor(n, sh, dt))
            g8g_i = sb2("g8g_i", [128, 3080], I32)
            g8g = sb2("g8g", [128, 3080], F16)
            OPS = []
            OPS.append(("gpsimd", lambda e: e.dma_start(
                out=g8g_i[:], in_=bass.AP(t_gigu, 0, [[8, 128], [1, 3080]]))))
            OPS.append(("vector", lambda e: e.tensor_copy(g8g[:], g8g_i[:])))

            def mk_ohgu(e):
                for kt in range(NKT):
                    qb, k = kt // 8, kt % 8
                    col = 1024 * qb + k
                    e.tensor_tensor(oPgu[:, 32 * kt:32 * kt + 32],
                                    g8g[:, col:col + 1].broadcast_to([128, 32]),
                                    g32[:], ALU.is_equal)
            OPS.append(("vector", mk_ohgu))
            for kt in range(NKT):
                OPS.append(("tensor", lambda e, kt=kt: e.transpose(
                    pT16[0:32, 0:128], oPgu[:, 32 * kt:32 * kt + 32], id128[:])))
                OPS.append(("vector", lambda e, kt=kt: e.tensor_copy(
                    oPTgu[0:32, 128 * kt:128 * kt + 128], pT16[0:32, 0:128])))
            KC_C, KC_D = chain_totals(KOPS)
            with nc.Block() as block:
                for en in ["gpsimd", "scalar", "tensor", "vector"]:
                    def mk(en, OPS=OPS):
                        def f(q):
                            q.wait_ge(s_kc, 16 * KC_C)
                            q.wait_ge(s_kcd, 16 * KC_D)
                            run_chain(en, q, OPS, s_tb, s_tbd, 0, 0)
                        return f
                    getattr(block, en)(mk(en))
            base2 = chain_totals(OPS)

        with ExitStack() as P2b:
            sb2 = lambda n, sh, dt: P2b.enter_context(nc.sbuf_tensor(n, sh, dt))
            g8d_i = sb2("g8d_i", [128, 10248], I32)
            g8d = sb2("g8d", [128, 10248], F16)
            OPS = []
            OPS.append(("gpsimd", lambda e: e.dma_start(
                out=g8d_i[0:96, :], in_=bass.AP(t_gid, 0, [[8, 96], [1, 10248]]))))
            OPS.append(("gpsimd", lambda e: e.dma_start(
                out=g8d_i[96:128, 0:9224], in_=bass.AP(t_gid, 768, [[8, 32], [1, 9224]]))))
            OPS.append(("vector", lambda e: e.memset(g8d[:], 0.0)))
            OPS.append(("vector", lambda e: e.tensor_copy(g8d[0:96, :], g8d_i[0:96, :])))
            OPS.append(("vector", lambda e: e.tensor_copy(g8d[96:128, 0:9224], g8d_i[96:128, 0:9224])))
            OPS[-2] = ("vector", lambda e: e.tensor_copy(g8d[0:64, :], g8d_i[0:64, :]))
            OPS[-1] = ("vector", lambda e: e.tensor_copy(g8d[64:128, 0:9224], g8d_i[64:128, 0:9224]))
            OPS.append(("vector", lambda e: e.tensor_copy(g8d[64:96, 9224:10248], g8d_i[64:96, 9224:10248])))

            def mk_ohd(e):
                for t in range(NPAIR):
                    k, b, w, co, fo = PAIRS[t]
                    col = 1024 * b + k
                    e.tensor_tensor(oPd[:, 86 * t:86 * t + 86],
                                    g8d[:, col:col + 1].broadcast_to([128, 86]),
                                    g86[:], ALU.is_equal)
            OPS.append(("vector", mk_ohd))
            with nc.Block() as block:
                for en in ["gpsimd", "scalar", "tensor", "vector"]:
                    def mk(en, OPS=OPS, base2=base2):
                        def f(q):
                            run_chain(en, q, OPS, s_tb, s_tbd, base2[0], base2[1])
                        return f
                    getattr(block, en)(mk(en))
            c3, d3 = chain_totals(OPS)
            base3 = (base2[0] + c3, base2[1] + d3)

        with ExitStack() as P2c:
            sb2 = lambda n, sh, dt: P2c.enter_context(nc.sbuf_tensor(n, sh, dt))
            qzs = sb2("qzs", [32, 2752], U32)
            zqk = sb2("zqk", [32, 2752], U32)
            scc = [sb2(f"scc{i}", [32, 2752], F32) for i in range(2)]
            szs = [sb2(f"szs{i}", [32, 2752], F16) for i in range(2)]
            OPS = []
            OPS.append(("gpsimd", lambda e: e.dma_start(out=qzs[:], in_=t_qzgu.ap())))
            for k in range(8):
                OPS.append(("gpsimd", lambda e, k=k: e.dma_start(
                    out=scc[k % 2][:], in_=t_scgu[0:32, 2752 * k:2752 * k + 2752])))
                OPS.append(("vector", lambda e, k=k: e.tensor_copy(
                    scgu16[0:32, 2752 * k:2752 * k + 2752], scc[k % 2][:])))
                OPS.append(("vector", lambda e, k=k: e.tensor_scalar(
                    zqk[:], qzs[:], 4 * k, 15, ALU.logical_shift_right, ALU.bitwise_and)))

                def mk_szp(e, k=k):
                    for side in range(2):
                        e.scalar_tensor_tensor(
                            apx(szs[k % 2][0:32, :], 128 * side, [[256, 10], [1, 128]]),
                            apx(zqk[0:32, :], 1376 * side, [[128, 10], [1, 128]]),
                            1.0,
                            apx(scc[k % 2][0:32, :], 128 * side, [[256, 10], [1, 128]]),
                            ALU.add, ALU.mult)
                        e.scalar_tensor_tensor(
                            apx(szs[k % 2][0:32, :], 2560 + 96 * side, [[1, 96]]),
                            apx(zqk[0:32, :], 1280 + 1376 * side, [[1, 96]]),
                            1.0,
                            apx(scc[k % 2][0:32, :], 2560 + 96 * side, [[1, 96]]),
                            ALU.add, ALU.mult)
                    return None
                OPS.append(("vector", mk_szp))
                OPS.append(("gpsimd", lambda e, k=k: e.dma_start(
                    out=t_szp[0:32, 2752 * k:2752 * k + 2752], in_=szs[k % 2][:])))
            if _DBG:
                OPS.append(("gpsimd", lambda e: e.dma_start(out=dbg["szp"].ap(), in_=t_szp.ap())))
            with nc.Block() as block:
                for en in ["gpsimd", "scalar", "tensor", "vector"]:
                    def mk(en, OPS=OPS, base3=base3):
                        def f(q):
                            run_chain(en, q, OPS, s_tb, s_tbd, base3[0], base3[1])
                        return f
                    getattr(block, en)(mk(en))
            c4, d4 = chain_totals(OPS)
            return (base3[0] + c4, base3[1] + d4)

    def emit_p3():
        with ExitStack() as P3:
            sb3 = lambda n, sh, dt: P3.enter_context(nc.sbuf_tensor(n, sh, dt))
            xt = [sb3(f"xt{i}", [128, H], F32) for i in range(2)]
            ssc = sb3("ssc", [128, H], F32)
            st1 = sb3("st1", [128, 8], F32)
            st2 = sb3("st2", [128, 8], F32)
            xnp16 = sb3("xnp16", [128, H], F16)
            with nc.Block() as block:
                @block.sync
                def _(sy):
                    if _SKIP12 == 0:
                        sy.wait_ge(s_tb, 16 * TB_TOTAL[0])
                        sy.wait_ge(s_tbd, 16 * TB_TOTAL[1])
                    elif _SKIP12 == 2:
                        KC = chain_totals(KOPS)
                        sy.wait_ge(s_kc, 16 * KC[0])
                        sy.wait_ge(s_kcd, 16 * KC[1])
                    else:
                        sy.wait_ge(s_ld, 16 * N_P0)
                    for st in range(8):
                        if st >= 2:
                            sy.wait_ge(s_ln, st - 1)
                            sy.wait_ge(s_la, st - 1)
                        sy.dma_start(out=xt[st % 2][:],
                                     in_=t_x[128 * st:128 * st + 128, :]).then_inc(s_lx, 16)

                @block.scalar
                def _(sc):
                    sc.wait_ge(s_lx, 16)
                    sc.activation(ssc[:], xt[0][:], AF.Square, accum_out=st2[:, 3:4])
                    sc.drain()
                    sc.activation(st2[:, 3:4], st2[:, 3:4], AF.Ln, bias=c_eps[:])
                    sc.drain()
                    sc.activation(st2[:, 3:4], st2[:, 3:4], AF.Exp, scale=c_nhalf[:])
                    sc.drain()
                    for st in range(8):
                        sc.wait_ge(s_lx, 16 * (st + 1))
                        if st >= 1:
                            sc.wait_ge(s_lv, 2 * st)
                        else:
                            sc.wait_ge(s_lv, 1)
                        sc.activation(ssc[:], xt[st % 2][:], AF.Square,
                                      accum_out=st2[:, 0:1])
                        sc.drain().then_inc(s_la, 1)
                        sc.wait_ge(s_lv, 2 * st + 2)
                        sc.activation(st2[:, 1:2], st1[:, 5:6], AF.Ln, bias=c_eps[:])
                        sc.drain()
                        sc.activation(st2[:, 2:3], st2[:, 1:2], AF.Exp, scale=c_nhalf[:])
                        sc.drain().then_inc(s_lr, 1)

                @block.vector
                def _(ve):
                    ve.wait_ge(s_lx, 16)
                    ve.reduce_sum(st1[:, 6:7].unsqueeze(2),
                                  xt[0][:].rearrange("p (a b) -> p a b", a=1, b=H), axis=AXX)
                    ve.drain()
                    ve.tensor_scalar(st1[:, 6:7], st1[:, 6:7], c_invh[:, 0:1], None, ALU.mult)
                    ve.drain()
                    for st in range(8):
                        ve.wait_ge(s_lx, 16 * (st + 1))
                        ve.reduce_sum(st1[:, 0:1].unsqueeze(2),
                                      xt[st % 2][:].rearrange("p (a b) -> p a b", a=1, b=H), axis=AXX)
                        ve.drain()
                        ve.tensor_scalar(st1[:, 1:2], st1[:, 0:1], c_invh[:, 0:1], None, ALU.mult)
                        ve.drain().then_inc(s_lv, 1)
                        ve.wait_ge(s_la, st + 1)
                        ve.tensor_scalar(st1[:, 2:3], st2[:, 0:1], c_invh[:, 0:1], None, ALU.mult)
                        ve.tensor_tensor(st1[:, 3:4], st1[:, 1:2], st1[:, 1:2], ALU.mult)
                        ve.drain()
                        ve.tensor_tensor(st1[:, 5:6], st1[:, 2:3], st1[:, 3:4], ALU.subtract)
                        ve.drain().then_inc(s_lv, 1)
                        ve.wait_ge(s_lr, st + 1)
                        if st >= 1 and _P3SUB >= 2:
                            ve.wait_ge(s_lp, 32 * st)
                        for qb4 in range(4):
                            ve.tensor_scalar(
                                xnp16[:, 1024 * qb4:1024 * qb4 + 1024].rearrange("p (b c) -> p b c", b=8, c=128),
                                apx(xt[st % 2][:], 1024 * qb4, [[1, 8], [8, 128]]),
                                st1[:, 1:2], st2[:, 2:3], ALU.subtract, ALU.mult)
                        ve.drain().then_inc(s_ln, 1)
                        if _P3SUB >= 2:
                            for jt in range(32):
                                gi = 32 * st + jt
                                ve.wait_ge(s_lp, gi + 1)
                                slot = (pT16[:, 0:128] if gi % 2 == 0
                                        else p[6][:, 0:64].bitcast(F16))
                                ve.tensor_copy(xnt[:, 1024 * jt + 128 * st:1024 * jt + 128 * st + 128],
                                               slot).then_inc(s_lc, 1)
                    if _P3SUB < 3 or _SKIP12:
                        return
                    ve.wait_ge(s_xg, 1)
                    ve.tensor_scalar(xgn[0:32, 0:512], p[0][0:32, :], -1.0, None, ALU.mult)
                    ve.tensor_scalar(xgn[0:32, 512:1024], p[1][0:32, :], -1.0, None, ALU.mult)
                    ve.drain().then_inc(s_xg, 1)

                @block.tensor
                def _(te):
                    if _P3SUB < 2:
                        return
                    for st in range(8):
                        te.wait_ge(s_ln, st + 1)
                        for jt in range(32):
                            gi = 32 * st + jt
                            if gi >= 2:
                                te.wait_ge(s_lc, gi - 1)
                            slot = (pT16[:, 0:128] if gi % 2 == 0
                                    else p[6][:, 0:64].bitcast(F16))
                            te.transpose(slot, xnp16[:, 128 * jt:128 * jt + 128], id128[:])
                            te.drain().then_inc(s_lp, 1)
                    if _P3SUB < 3:
                        return
                    te.wait_ge(s_lc, 256)
                    if _SKIP12:
                        return
                    for kt in range(NKT):
                        te.matmul(p[0][0:32, :], oPgu[:, 32 * kt:32 * kt + 32],
                                  xnt[:, 1024 * kt:1024 * kt + 512],
                                  start=(kt == 0), stop=(kt == NKT - 1))
                        te.matmul(p[1][0:32, :], oPgu[:, 32 * kt:32 * kt + 32],
                                  xnt[:, 1024 * kt + 512:1024 * kt + 1024],
                                  start=(kt == 0), stop=(kt == NKT - 1))
                    te.drain().then_inc(s_xg, 1)

    def emit_p4():
        with ExitStack() as P4:
            sb4 = lambda n, sh, dt: P4.enter_context(nc.sbuf_tensor(n, sh, dt))
            qwg = [sb4(f"qwg{i}", [128, 256], U32) for i in range(2)]
            plg = [sb4(f"plg{i}", [128, 256], U32) for i in range(2)]
            wtg = [sb4(f"wtg{i}", [128, 256], F16) for i in range(3)]
            htb = [sb4(f"htb{i}", [128, S], F16) for i in range(2)]
            hgb = sb4("hgb", [128, S], F16)
            szp16 = [sb4(f"szp16{i}", [32, 256], F16) for i in range(2)]
            NG = NPAIR * NKT

            with nc.Block() as block:
                @block.sync
                def _(sy):
                    sy.wait_ge(s_xg, 1)
                    for t in range(NPAIR):
                        k, b, w, co, fo = PAIRS[t]
                        for qb in range(4):
                            gq = 4 * t + qb
                            if gq >= 2:
                                sy.wait_ge(g_w, 8 * (gq - 1))
                            sy.dma_start(out=qwg[gq % 2][:, 0:2 * w],
                                         in_=t_qwgu[128 * qb:128 * qb + 128, co:co + 2 * w]).then_inc(g_qw, 16)

                @block.gpsimd
                def _(g):
                    g.wait_ge(s_xg, 1)
                    for t in range(NPAIR):
                        k, b, w, co, fo = PAIRS[t]
                        if t >= 2:
                            g.wait_ge(g_pe, t - 1)
                        g.dma_start(out=szp16[t % 2][0:32, 0:2 * w],
                                    in_=t_szp[0:32, co:co + 2 * w]).then_inc(g_szp, 16)

                @block.scalar
                def _(sc):
                    sc.wait_ge(s_xg, 1)
                    sc.activation(hgb[0:1, 0:1], c_eps[0:1, 0:1], AF.Silu)
                    sc.drain()
                    for t in range(NPAIR):
                        k, b, w, co, fo = PAIRS[t]
                        sc.wait_ge(g_pe, t + 1)
                        if t >= 1:
                            sc.wait_ge(g_h, t)
                        sc.activation(hgb[0:w, 0:512], p[0][0:w, :], AF.Silu)
                        sc.activation(hgb[0:w, 512:1024], p[1][0:w, :], AF.Silu)
                        sc.drain().then_inc(g_act, 1)
                        sc.wait_ge(g_h, t + 1)
                        sc.dma_start(out=bass.AP(t_h, 128 * S * t, [[S, w], [1, S]]),
                                     in_=htb[t % 2][0:w, :]).then_inc(g_hst, 16)

                @block.tensor
                def _(te):
                    te.wait_ge(s_tb, 16 * TB_TOTAL[0])
                    te.wait_ge(s_tbd, 16 * TB_TOTAL[1])
                    te.wait_ge(s_xg, 2)
                    k0, b0, w0, co0, fo0 = PAIRS[0]
                    te.matmul(p[4][:, 0:2 * w0], oPTgu[0:32, 0:128],
                              scgu16[0:32, co0:co0 + 2 * w0], start=True, stop=True)
                    te.drain().then_inc(g_sce, 1)
                    for t in range(NPAIR):
                        k, b, w, co, fo = PAIRS[t]
                        for kt in range(NKT):
                            gk = NKT * t + kt
                            te.wait_ge(g_w, gk + 1)
                            if gk + 1 < NG:
                                t2, kt2 = divmod(gk + 1, NKT)
                                k_, b_, w2, co2, fo2 = PAIRS[t2]
                                te.matmul(p[4][:, 0:2 * w2], oPTgu[0:32, 128 * kt2:128 * kt2 + 128],
                                          scgu16[0:32, co2:co2 + 2 * w2],
                                          start=True, stop=True)
                            wt = wtg[gk % 3]
                            mm1 = te.matmul(p[0][0:w, :], wt[:, 0:w],
                                            xnt[:, 1024 * kt:1024 * kt + 512], start=(kt == 0), stop=False)
                            if gk + 1 < NG:
                                mm1.then_inc(g_sce, 1)
                            te.matmul(p[1][0:w, :], wt[:, 0:w],
                                      xnt[:, 1024 * kt + 512:1024 * kt + 1024], start=(kt == 0), stop=False)
                            te.matmul(p[2][0:w, :], wt[:, w:2 * w],
                                      xnt[:, 1024 * kt:1024 * kt + 512], start=(kt == 0), stop=False)
                            te.matmul(p[3][0:w, :], wt[:, w:2 * w],
                                      xnt[:, 1024 * kt + 512:1024 * kt + 1024],
                                      start=(kt == 0), stop=False).then_inc(g_ld, 1)
                        te.wait_ge(g_szp, 16 * (t + 1))
                        te.matmul(p[0][0:w, :], szp16[t % 2][0:32, 0:w], xgn[0:32, 0:512],
                                  start=False, stop=True)
                        te.matmul(p[1][0:w, :], szp16[t % 2][0:32, 0:w], xgn[0:32, 512:1024],
                                  start=False, stop=True)
                        te.matmul(p[2][0:w, :], szp16[t % 2][0:32, w:2 * w], xgn[0:32, 0:512],
                                  start=False, stop=True)
                        te.matmul(p[3][0:w, :], szp16[t % 2][0:32, w:2 * w], xgn[0:32, 512:1024],
                                  start=False, stop=True)
                        te.drain().then_inc(g_pe, 1)
                        te.wait_ge(g_h, t + 1)
                        te.matmul(p[5][0:86, :], oPd[0:w, 86 * t:86 * t + 86], htb[t % 2][0:w, 0:512],
                                  start=(t == 0), stop=(t == NPAIR - 1))
                        te.matmul(p[6][0:86, :], oPd[0:w, 86 * t:86 * t + 86], htb[t % 2][0:w, 512:1024],
                                  start=(t == 0), stop=(t == NPAIR - 1)).then_inc(g_xgd, 1)
                    te.drain().then_inc(g_xgd, 1)

                @block.vector
                def _(ve):
                    ve.wait_ge(s_tb, 16 * TB_TOTAL[0])
                    ve.wait_ge(s_tbd, 16 * TB_TOTAL[1])
                    for t in range(NPAIR):
                        k, b, w, co, fo = PAIRS[t]
                        for kt in range(NKT):
                            gk = NKT * t + kt
                            qb, kk = kt // 8, kt % 8
                            gq = 4 * t + qb
                            if kk == 0:
                                ve.wait_ge(g_qw, 16 * (gq + 1))
                            ve.tensor_scalar(plg[gk % 2][:, 0:2 * w], qwg[gq % 2][:, 0:2 * w],
                                             4 * kk, 15, ALU.logical_shift_right, ALU.bitwise_and)
                            ve.drain()
                            ve.wait_ge(g_sce, gk + 1)
                            if gk >= 3:
                                ve.wait_ge(g_ld, gk - 2)
                            ve.tensor_tensor(wtg[gk % 3][:, 0:2 * w], plg[gk % 2][:, 0:2 * w],
                                             p[4][:, 0:2 * w], ALU.mult)
                            ve.drain().then_inc(g_w, 1)
                        ve.wait_ge(g_act, t + 1)
                        if t >= 2:
                            ve.wait_ge(g_hst, 16 * (t - 1))
                            ve.wait_ge(g_xgd, t - 1)
                        ve.tensor_tensor(htb[t % 2][0:w, 0:512], hgb[0:w, 0:512], p[2][0:w, :], ALU.mult)
                        ve.tensor_tensor(htb[t % 2][0:w, 512:1024], hgb[0:w, 512:1024],
                                         p[3][0:w, :], ALU.mult)
                        ve.drain().then_inc(g_h, 1)
                    ve.wait_ge(g_xgd, NPAIR + 1)
                    ve.wait_ge(g_hst, 16 * NPAIR)
                    ve.tensor_scalar(xgdn[0:86, 0:512], p[5][0:86, :], -1.0, None, ALU.mult)
                    ve.tensor_scalar(xgdn[0:86, 512:1024], p[6][0:86, :], -1.0, None, ALU.mult)
                    ve.drain().then_inc(s_xg, 1)

    def emit_p5p6():
        with ExitStack() as P6:
            sb6 = lambda n, sh, dt: P6.enter_context(nc.sbuf_tensor(n, sh, dt))
            oPTd = sb6("oPTd", [86, INTER], F16)
            scd16 = sb6("scd16", [86, H], F16)
            szpd = sb6("szpd", [86, H], F16)

            OPS = []
            for t in range(NPAIR):
                k, b, w, co, fo = PAIRS[t]
                OPS.append(("tensor", lambda e, t=t, w=w: e.transpose(
                    pT16[0:86, 0:w], oPd[0:w, 86 * t:86 * t + 86], id128[0:w, 0:w])))
                OPS.append(("vector", lambda e, t=t, w=w, fo=fo: e.tensor_copy(
                    oPTd[0:86, fo:fo + w], pT16[0:86, 0:w])))
            N_T5 = len(OPS)

            def emit_p5():
                with ExitStack() as P5:
                    sb5 = lambda n, sh, dt: P5.enter_context(nc.sbuf_tensor(n, sh, dt))
                    qzd = sb5("qzd", [86, 512], U32)
                    zqd = sb5("zqd", [86, H], U32)
                    OPS5 = list(OPS)
                    OPS5.append(("gpsimd", lambda e: e.dma_start(out=qzd[:], in_=t_qzd.ap())))
                    OPS5.append(("gpsimd", lambda e: e.dma_start(out=scd16[:], in_=t_scd.ap())))

                    def unz(e):
                        for k2 in range(8):
                            e.tensor_scalar(apx(zqd[0:86, :], k2, [[8, 512]]),
                                            qzd[:], 4 * k2, 15,
                                            ALU.logical_shift_right, ALU.bitwise_and)
                    OPS5.append(("vector", unz))
                    OPS5.append(("vector", lambda e: e.scalar_tensor_tensor(
                        apx(szpd[0:86, :], 127, [[128, 32], [-1, 128]]),
                        zqd[0:86, :].rearrange("p (a b) -> p a b", a=32, b=128),
                        1.0,
                        scd16[0:86, :].rearrange("p (a b) -> p a b", a=32, b=128),
                        ALU.add, ALU.mult)))
                    with nc.Block() as block:
                        for en in ["gpsimd", "scalar", "tensor", "vector"]:
                            def mk(en, OPS5=OPS5):
                                def f(q):
                                    q.wait_ge(s_xg, 3)
                                    run_chain(en, q, OPS5, s_p5, s_p5d, 0, 0)
                                return f
                            getattr(block, en)(mk(en))
                    return chain_totals(OPS5)

            N_P5 = emit_p5()
            RS.close()

            slab = sb6("slab", [128, 32 * 512], F16)
            qwd = [sb6(f"qwd{i}", [128, 512], U32) for i in range(2)]
            pld = [sb6(f"pld{i}", [128, 512], U32) for i in range(2)]
            wtdA = [sb6(f"wtdA{i}", [128, 512], F16) for i in range(3)]
            wtdB = [sb6(f"wtdB{i}", [128, 512], F16) for i in range(3)]
            htd = [sb6(f"htd{i}", [128, 512], F16) for i in range(3)]
            ysc = [sb6(f"ysc{i}", [128, 512], F32) for i in range(4)]
            yst = [sb6(f"yst{i}", [128, 512], F32) for i in range(4)]
            xpc = [sb6(f"xpc{i}", [128, 512], F32) for i in range(2)]
            opc = [sb6(f"opc{i}", [128, 512], F32) for i in range(2)]

            def wd_off(grp, t):
                return (grp * NPAIR + t) * 128 * 512

            with nc.Block() as block:
                @block.sync
                def _(sy):
                    sy.wait_ge(s_p5, 16 * N_P5[0])
                    sy.wait_ge(s_p5d, 16 * N_P5[1])
                    for grp in range(NGRP):
                        for half in range(2):
                            pi = 2 * grp + half
                            for t in range(NPAIR):
                                k, b, w, co, fo = PAIRS[t]
                                if half == 0:
                                    iA = grp * NPAIR + t
                                    if iA >= 2:
                                        sy.wait_ge(d_w, iA - 1)
                                    sy.dma_start(out=qwd[t % 2][0:w, :],
                                                 in_=t_qwd[128 * b:128 * b + w,
                                                           512 * grp:512 * grp + 512]).then_inc(d_qw, 16)
                                else:
                                    iB = grp * NPAIR + t
                                    sy.wait_ge(d_ws, 16 * (iB + 1))
                                    if iB >= 3:
                                        t3, g3 = (iB - 3) % NPAIR, (iB - 3) // NPAIR
                                        sy.wait_ge(d_cons, (2 * g3 + 1) * NPAIR + t3 + 1)
                                    sy.dma_start(out=wtdB[iB % 3][0:w, :],
                                                 in_=bass.AP(t_wd, wd_off(grp, t),
                                                             [[512, w], [1, 512]])).then_inc(d_wl, 16)

                @block.gpsimd
                def _(g):
                    g.wait_ge(s_p5, 16 * N_P5[0])
                    g.wait_ge(s_p5d, 16 * N_P5[1])
                    for grp in range(NGRP):
                        if grp >= 1:
                            g.wait_ge(d_pe, 4 * grp - 1)
                        for (qb, k) in slabs_of_grp(grp):
                            base = conv_base(qb, k, 4 * grp + 3)
                            g.dma_start(out=slab[:, 512 * (8 * qb + k):512 * (8 * qb + k) + 512],
                                        in_=bass.AP(t_kr, base, [[8, 128], [1, 512]])).then_inc(d_sl, 16)
                        for half in range(2):
                            pi = 2 * grp + half
                            for c in range(4):
                                i = 4 * pi + c
                                if i >= 2:
                                    g.wait_ge(d_oc, i - 1)
                                g.dma_start(out=xpc[i % 2][:],
                                            in_=t_x[512 * half + 128 * c:512 * half + 128 * c + 128,
                                                    512 * grp:512 * grp + 512]).then_inc(s_xr, 16)
                                g.wait_ge(d_oc, i + 1)
                                g.dma_start(out=t_out[512 * half + 128 * c:512 * half + 128 * c + 128,
                                                      512 * grp:512 * grp + 512],
                                            in_=opc[i % 2][:]).then_inc(s_out, 16)
                    g.wait_ge(s_out, 16 * NGRP * 2 * 4)
                    if _DBG:
                        g.dma_start(out=dbg["xnt"].ap(), in_=xnt[:]).then_inc(s_out, 16)
                        g.dma_start(out=dbg["xgn"].ap(), in_=xgn[:]).then_inc(s_out, 16)
                        g.dma_start(out=dbg["xgdn"].ap(), in_=xgdn[:]).then_inc(s_out, 16)
                        g.dma_start(out=dbg["h"].ap(),
                                    in_=bass.AP(t_h, 0, [[128 * S, 1], [1, 128 * S]])).then_inc(s_out, 16)
                        g.wait_ge(s_out, 16 * (NGRP * 2 * 4 + 4))

                @block.scalar
                def _(sc):
                    sc.wait_ge(s_p5, 16 * N_P5[0])
                    sc.wait_ge(s_p5d, 16 * N_P5[1])
                    for grp in range(NGRP):
                        for half in range(2):
                            pi = 2 * grp + half
                            for t in range(NPAIR):
                                k, b, w, co, fo = PAIRS[t]
                                i = pi * NPAIR + t
                                if i >= 3:
                                    sc.wait_ge(d_cons, i - 2)
                                sc.dma_start(out=htd[i % 3][0:w, :],
                                             in_=bass.AP(t_h, 128 * S * t + 512 * half,
                                                         [[S, w], [1, 512]])).then_inc(d_h, 16)
                                if half == 0:
                                    iA = grp * NPAIR + t
                                    sc.wait_ge(d_w, iA + 1)
                                    sc.dma_start(out=bass.AP(t_wd, wd_off(grp, t), [[512, w], [1, 512]]),
                                                 in_=wtdA[iA % 3][0:w, :]).then_inc(d_ws, 16)
                    nld = 2 * NGRP * NPAIR
                    sc.wait_ge(d_cons, nld - 2)
                    sc.dma_start(out=htd[nld % 3][0:128, :],
                                 in_=bass.AP(t_h, 0, [[S, 128], [1, 512]])).then_inc(d_h, 16)

                @block.tensor
                def _(te):
                    te.wait_ge(s_p5, 16 * N_P5[0])
                    te.wait_ge(s_p5d, 16 * N_P5[1])
                    for grp in range(NGRP):
                        for half in range(2):
                            pi = 2 * grp + half
                            te.wait_ge(d_sl, 16 * NSLAB_CUM[grp + 1])
                            if pi >= 1:
                                te.wait_ge(d_ev, 4 * pi)
                                te.wait_ge(d_tc, 16 * pi)
                            nA_done = grp + (1 if half == 1 else 0)
                            if nA_done >= 1:
                                te.wait_ge(d_w, NPAIR * nA_done)
                            conv_mms = [[] for _ in range(4)]
                            for (qb, k) in slabs_of_grp(grp):
                                for sub in range(4):
                                    it = 4 * grp + sub
                                    if conv_valid(qb, it):
                                        conv_mms[sub].append((qb, k))
                            for sub in range(4):
                                it = 4 * grp + sub
                                n_mm = len(conv_mms[sub])
                                for j, (qb, k) in enumerate(conv_mms[sub]):
                                    sl = 512 * (8 * qb + k) + 128 * (4 * grp + 3 - it)
                                    te.matmul(p[3 + sub][:, :], slab[:, sl:sl + 128],
                                              xnt[:, 1024 * (8 * qb + k) + 512 * half:
                                                  1024 * (8 * qb + k) + 512 * half + 512],
                                              start=(j == 0), stop=(j == n_mm - 1))
                            te.drain().then_inc(d_pe, 1)
                            te.wait_ge(d_cv, 4 * pi + 4)
                            if half == 0:
                                kf, bf, wf, cof, fof = PAIRS[0]
                                te.matmul(p[4][0:wf, :], oPTd[0:86, fof:fof + wf],
                                          scd16[0:86, 512 * grp:512 * grp + 512],
                                          start=True, stop=True)
                                te.drain().then_inc(d_sce, 1)
                            for t in range(NPAIR):
                                k, b, w, co, fo = PAIRS[t]
                                i = pi * NPAIR + t
                                if half == 0:
                                    iA = grp * NPAIR + t
                                    te.wait_ge(d_w, iA + 1)
                                    emit_sce2 = t + 1 < NPAIR
                                    wt = wtdA[iA % 3]
                                    if emit_sce2:
                                        k2, b2, w2, co2, fo2 = PAIRS[t + 1]
                                        te.matmul(p[4][0:w2, :], oPTd[0:86, fo2:fo2 + w2],
                                                  scd16[0:86, 512 * grp:512 * grp + 512],
                                                  start=True, stop=True)
                                else:
                                    emit_sce2 = False
                                    te.wait_ge(d_wl, 16 * (grp * NPAIR + t + 1))
                                    wt = wtdB[(grp * NPAIR + t) % 3]
                                te.wait_ge(d_h, 16 * (i + 2))
                                for sub in range(4):
                                    mm = te.matmul(p[sub][:, :], wt[0:w, 128 * sub:128 * sub + 128],
                                                   htd[i % 3][0:w, :], start=(t == 0), stop=False)
                                    if sub == 0 and emit_sce2:
                                        mm.then_inc(d_sce, 1)
                                    if sub == 3:
                                        mm.then_inc(d_cons, 1)
                            for sub in range(4):
                                te.matmul(p[sub][:, :],
                                          szpd[0:86, 512 * grp + 128 * sub:512 * grp + 128 * sub + 128],
                                          xgdn[0:86, 512 * half:512 * half + 512],
                                          start=False, stop=True)
                            te.drain().then_inc(d_pe, 1)
                            for c in range(4):
                                for sub in range(4):
                                    gi = 16 * pi + 4 * c + sub
                                    te.wait_ge(d_ev, 4 * pi + sub + 1)
                                    if gi >= 2:
                                        te.wait_ge(d_tc, gi - 1)
                                    te.transpose(p[5 + (gi % 2)][:, 0:128],
                                                 yst[sub][:, 128 * c:128 * c + 128],
                                                 id128f[:])
                                    te.drain().then_inc(d_tp, 1)

                @block.vector
                def _(ve):
                    ve.wait_ge(s_p5, 16 * N_P5[0])
                    ve.wait_ge(s_p5d, 16 * N_P5[1])
                    for grp in range(NGRP):
                        for half in range(2):
                            pi = 2 * grp + half
                            ve.wait_ge(d_pe, 2 * pi + 1)
                            for sub in range(4):
                                ve.tensor_scalar(ysc[sub][:], p[3 + sub][:, :], 1.0 / SCALE_K,
                                                 None, ALU.mult)
                                ve.drain().then_inc(d_cv, 1)
                            if half == 0:
                                for t in range(NPAIR):
                                    k, b, w, co, fo = PAIRS[t]
                                    iA = grp * NPAIR + t
                                    ve.wait_ge(d_qw, 16 * (iA + 1))
                                    ve.tensor_scalar(pld[iA % 2][0:w, :], qwd[t % 2][0:w, :],
                                                     4 * k, 15, ALU.logical_shift_right, ALU.bitwise_and)
                                    ve.drain()
                                    ve.wait_ge(d_sce, iA + 1)
                                    if iA >= 3:
                                        t3, g3 = (iA - 3) % NPAIR, (iA - 3) // NPAIR
                                        ve.wait_ge(d_cons, 2 * g3 * NPAIR + t3 + 1)
                                        ve.wait_ge(d_ws, 16 * (iA - 2))
                                    ve.tensor_tensor(
                                        apx(wtdA[iA % 3][0:w, :], 127, [[128, 4], [-1, 128]]),
                                        pld[iA % 2][0:w, :], p[4][0:w, :], ALU.mult)
                                    ve.drain()
                                    ve.tensor_copy(smf[0:64, 0:1],
                                                   wtdA[iA % 3][0:64, 384:385])
                                    ve.drain().then_inc(d_w, 1)
                            ve.wait_ge(d_pe, 2 * pi + 2)
                            if pi >= 1:
                                ve.wait_ge(d_tp, 16 * pi)
                            for sub in range(4):
                                ve.tensor_tensor(yst[sub][:], p[sub][:, :], ysc[sub][:], ALU.add)
                                ve.drain().then_inc(d_ev, 1)
                            for c in range(4):
                                i = 4 * pi + c
                                ve.wait_ge(s_xr, 16 * (i + 1))
                                if i >= 2:
                                    ve.wait_ge(s_out, 16 * (i - 1))
                                for sub in range(4):
                                    gi = 16 * pi + 4 * c + sub
                                    ve.wait_ge(d_tp, gi + 1)
                                    ve.tensor_tensor(
                                        opc[i % 2][:, 128 * sub:128 * sub + 128],
                                        xpc[i % 2][:, 128 * sub:128 * sub + 128],
                                        apx(p[5 + (gi % 2)][:, :], 127, [[-1, 128]]),
                                        ALU.add)
                                    ve.drain().then_inc(d_tc, 1)
                                ve.drain().then_inc(d_oc, 1)

    KC_TOT = chain_totals(KOPS)
    if _SKIP12 == 1:
        emit_p0_only()
    else:
        emit_p0p1()
    P1.close()
    TB_TOTAL = (0, 0)
    if _PHASES >= 2 and _SKIP12 == 0:
        TB_TOTAL = emit_p2()
    if _PHASES >= 3:
        emit_p3()
    if _PHASES >= 4:
        emit_p4()
    if _PHASES >= 6:
        emit_p5p6()
    if _PHASES < 6:
        with nc.Block() as block:
            @block.gpsimd
            def _(g):
                if _PHASES >= 1 and _SKIP12 != 1:
                    g.wait_ge(s_kc, 16 * KC_TOT[0])
                    g.wait_ge(s_kcd, 16 * KC_TOT[1])
                if _PHASES >= 2:
                    g.wait_ge(s_tb, 16 * TB_TOTAL[0])
                    g.wait_ge(s_tbd, 16 * TB_TOTAL[1])
                if _PHASES >= 3:
                    if _SKIP12:
                        g.wait_ge(s_lc, 256)
                    elif _P3SUB >= 3:
                        g.wait_ge(s_xg, 2)
                    elif _P3SUB == 2:
                        g.wait_ge(s_lc, 256)
                    else:
                        g.wait_ge(s_ln, 8)
                        g.wait_ge(s_la, 8)
                if _PHASES >= 4:
                    g.wait_ge(s_xg, 3)
                    g.wait_ge(g_hst, 16 * NPAIR)
                for r8 in range(8):
                    g.dma_start(out=t_out[128 * r8:128 * r8 + 128, 0:2048],
                                in_=xnt[:, 4096 * r8:4096 * r8 + 4096].bitcast(F32)).then_inc(s_out, 16)
                g.wait_ge(s_out, 16 * 8)
    TOP.close()
    return nc, dbg


def kernel(**inputs):
    xsh, d = host_prep(inputs)
    nc, _ = build_nc()
    in_maps = []
    for c in range(NCORE):
        m = {"x": xsh[c]}
        m.update(d)
        in_maps.append(m)
    res = run_bass_kernel_spmd(nc, in_maps, core_ids=list(range(NCORE)))
    out = np.concatenate([np.asarray(res.results[c]["out"]) for c in range(NCORE)], axis=0)
    return out.reshape(1, 8192, H).astype(np.float32)
